# version 17
# baseline (speedup 1.0000x reference)
import contextlib
import numpy as np
import concourse.bass as bass
import concourse.mybir as mybir
from concourse.bass_utils import run_bass_kernel_spmd

F32 = mybir.dt.float32
BF16 = mybir.dt.bfloat16
AF = mybir.ActivationFunctionType
ALU = mybir.AluOpType

L, D, H = 8192, 1024, 4096
TT = 1024
NT = L // TT
DC, HC = D // 128, H // 128
NCORES = 8

ENGS = ("sync", "scalar", "vector", "tensor", "gpsimd")
SEMS = tuple(["sa", "sb", "gq", "pe", "sc", "ve", "po", "wrd"]
             + [f"ws{k}" for k in range(12)] + [f"gs{k}" for k in range(8)])


class Prog:
    def __init__(self):
        self.stream = {e: [] for e in ENGS}
        self.cnt = {s: 0 for s in SEMS}

    def w(self, eng, sem, val, defer=False):
        if val is None or val <= 0:
            return
        assert defer or val <= self.cnt[sem], (
            f"wait {sem}>={val} on {eng} but only {self.cnt[sem]} planned")
        st = self.stream[eng]
        if st and st[-1][0] == "wait" and st[-1][1] == sem:
            if st[-1][2] >= val:
                return
            st[-1] = ("wait", sem, val)
            return
        st.append(("wait", sem, val))

    def i(self, eng, fn, inc=None, n=1):
        if inc is None:
            self.stream[eng].append(("op", fn, None, 0))
            return None
        self.cnt[inc] += n
        self.stream[eng].append(("op", fn, inc, n))
        return self.cnt[inc]


def build(phases=("f1", "b", "f2"), dbg=False):
    nc = bass.Bass("TRN2", target_bir_lowering=False)
    P = Prog()

    xT = nc.dram_tensor("xT", [D, L], F32, kind="ExternalInput").ap()
    outT = nc.dram_tensor("outT", [D, L], F32, kind="ExternalOutput").ap()
    WSPEC = [("wq", D, D), ("wk", D, D), ("wv", D, D), ("wu", D, D),
             ("wo", D, D), ("w1", D, H), ("w2", D, H), ("h1", H, H),
             ("h2", H, H), ("w3", H, D)]
    wt = {nm: nc.dram_tensor(nm + "_t", [I, O], F32, kind="ExternalInput").ap()
          for nm, I, O in WSPEC}
    SVSPEC = [("s1", HC), ("s2", HC), ("h1s", HC), ("h2s", HC), ("w3s", DC)]
    sv_in = {nm: nc.dram_tensor(nm + "_r", [128, n], F32, kind="ExternalInput").ap()
             for nm, n in SVSPEC}
    SWZ = ("w1", "w2", "h1", "h2", "w3", "wq", "wu", "wo")
    wz = {nm: nc.dram_tensor(nm + "_z", [O // 128, 128, I // 128, 128], BF16,
                             kind="Internal").ap()
          for nm, I, O in WSPEC if nm in SWZ}
    wp = {nm: nc.dram_tensor(nm + "_p", [D, D], BF16, kind="Internal").ap()
          for nm in ("wk", "wv")}
    x1_scr = nc.dram_tensor("x1_scr", [D, L], F32, kind="Internal").ap()
    x2_scr = nc.dram_tensor("x2_scr", [D, L], F32, kind="Internal").ap()
    ln_scr = nc.dram_tensor("ln_scr", [D, L], BF16, kind="Internal").ap()

    al = lambda nm, sh, dt: nc.alloc_sbuf_tensor(nm, sh, dt).ap()
    A3 = al("A3", [128, 32, 1024], BF16)
    B3 = al("B3", [128, 32, 1024], BF16)
    Xs = al("Xs", [128, 8, 512], F32)
    Lb = al("Lb", [128, 8, 1024], BF16)
    xbf = al("xbf", [128, 8, 512], BF16)
    x2b = al("x2b", [128, 8, 512], BF16)
    Ws = al("Ws", [128, 2, 4096], BF16)
    Sm = al("Sm", [128, 3, 512], F32)
    ones_bf = al("ones_bf", [128, 128], BF16)
    bdo = al("bdo", [128, 128], BF16)
    eps5 = al("eps5", [128, 1], F32)
    eps24 = al("eps24", [128, 1], F32)
    svb = {nm: al("sv_" + nm, [128, n], F32) for nm, n in SVSPEC}
    halfs3 = al("halfs3", [128, DC], F32)
    fscr = al("fscr", [128, 512], F32)
    vstage = al("vstage", [128, 1024], F32)
    vout = al("vout", [128, 1024], BF16)
    kv_bd = xbf.rearrange("p a b -> p (a b)")[:, 0:1024]
    att2 = x2b.rearrange("p a b -> p (a b)")[:, 0:512]

    pbw = [nc.alloc_psum_tensor(f"pbw{k}", [128, 1024], F32).ap() for k in range(4)]
    pb = [pbw[k // 2][:, (k % 2) * 512:(k % 2 + 1) * 512] for k in range(8)]

    bank_evac = [("pe", 0)] * 8
    bank_mm = [0] * 8
    wbuf_free2 = [0, 0, 0, 0]
    w3ring_free = [0, 0, 0, 0]

    def init_consts():
        P.i("vector", lambda e: e.memset(ones_bf, 1.0))
        P.i("vector", lambda e: e.memset(bdo[0:64, 0:64], 1.0))
        P.i("vector", lambda e: e.memset(bdo[0:64, 64:128], 0.0))
        P.i("vector", lambda e: e.memset(bdo[64:128, 0:64], 0.0))
        P.i("vector", lambda e: e.memset(bdo[64:128, 64:128], 1.0))
        P.i("vector", lambda e: e.memset(eps5, 1e-5))
        P.i("vector", lambda e: e.memset(eps24, 1e-24), inc="ve")
        for nm, _n in SVSPEC:
            P.i("gpsimd", lambda e, a=svb[nm], b=sv_in[nm]:
                e.dma_start(out=a, in_=b), inc="gq", n=16)
        svg = P.cnt["gq"]
        P.w("scalar", "gq", svg)
        P.i("scalar", lambda e: e.activation(out=halfs3, in_=svb["w3s"],
                                             func=AF.Copy, scale=0.5), inc="sc")
        P.w("vector", "gq", svg)
        P.w("gpsimd", "gq", svg)

    W_ORDER = ["w1", "w2", "h1", "h2", "w3", "wq", "wu", "wo", "wk", "wv"]
    _wdims = {nm: (I, O) for nm, I, O in WSPEC}
    WITEMS = []
    for _nm in W_ORDER:
        _I, _O = _wdims[_nm]
        for _ic in range(_I // 128):
            for _ob in range(_O // 512):
                WITEMS.append((_nm, _ic, _ob,
                               _ic == _I // 128 - 1 and _ob == _O // 512 - 1))
    wst = {"pos": 0, "cve": [None] * 12, "t": 0}
    wvin = [B3[:, 14 + k, :].bitcast(F32) for k in range(12)]
    wvo = [B3[:, 26 + k // 2, (k % 2) * 512:(k % 2 + 1) * 512] for k in range(8)]
    wready = {nm: i + 1 for i, nm in enumerate(W_ORDER)}

    def wpump(k):
        while k > 0 and wst["pos"] < len(WITEMS):
            nm, ic, ob, last = WITEMS[wst["pos"]]
            t = wst["t"]
            si, so = t % 12, t % 8
            wsem, gsem = f"ws{si}", f"gs{so}"
            if wst["cve"][si] is not None:
                P.w("sync", wst["cve"][si][0], wst["cve"][si][1])
            srcw = wt[nm][ic * 128:(ic + 1) * 128, ob * 512:(ob + 1) * 512]
            ld = P.i("sync", lambda e, o=wvin[si], s=srcw:
                     e.dma_start(out=o, in_=s), inc=wsem, n=16)
            P.w("vector", wsem, ld)
            P.w("vector", gsem, P.cnt[gsem])
            if nm == "h1":
                cv = P.i("vector", lambda e, o=wvo[so], s=wvin[si], c=ic:
                         e.tensor_scalar(out=o, in0=s,
                                         scalar1=svb["s2"][:, c:c + 1],
                                         scalar2=None, op0=ALU.mult), inc="ve")
            else:
                cv = P.i("vector", lambda e, o=wvo[so], s=wvin[si]:
                         e.tensor_copy(o, s), inc="ve")
            wst["cve"][si] = ("ve", cv)
            P.w("gpsimd", "ve", cv)
            if nm in SWZ:
                dst = wz[nm][ob * 4:(ob + 1) * 4, :, ic, :].transpose([1, 0, 2])
                ssrc = wvo[so].rearrange("p (f1 f2) -> p f1 f2", f1=4)
            else:
                dst = wp[nm][ic * 128:(ic + 1) * 128, ob * 512:(ob + 1) * 512]
                ssrc = wvo[so]
            P.i("gpsimd", lambda e, o=dst, s=ssrc:
                e.dma_start(out=o, in_=s), inc=gsem, n=16)
            if last:
                for so2 in range(8):
                    P.w("gpsimd", f"gs{so2}", P.cnt[f"gs{so2}"])
                P.i("gpsimd", lambda e: e.engine_nop(), inc="wrd")
            wst["pos"] += 1
            wst["t"] += 1
            k -= 1

    pipe = {"stats_free": (("pe", 0), ("pe", 0)), "xsq_pe": 0, "sm_ve": 0,
            "fscr_ve": 0}

    def ln_load(srcap, tc, gate, xs=None):
        xs = Xs if xs is None else xs
        P.w("gpsimd", gate[0], gate[1])
        g = None
        for h in range(4):
            g = P.i("gpsimd", lambda e, tc=tc, h=h, s=srcap, xs=xs:
                    e.dma_start(out=xs[:, 2 * h:2 * h + 2, :],
                                in_=s[h * 256:(h + 1) * 256,
                                      tc:tc + 512].rearrange(
                                    "(c p) t -> p c t", p=128)), inc="gq", n=16)
        return g

    def ln_prep(gv, xs=None, pool=False):
        xs = Xs if xs is None else xs
        if pool:
            P.w("gpsimd", "pe", pipe["xsq_pe"])
            P.w("gpsimd", "gq", gv)
            pv = None
            for c in range(DC):
                P.i("gpsimd", lambda e, c=c, xs=xs:
                    e.tensor_mul(x2b[:, c, :], xs[:, c, :], xs[:, c, :]))
                pv = P.i("gpsimd", lambda e, c=c, xs=xs:
                         e.tensor_copy(xbf[:, c, :], xs[:, c, :]),
                         inc="po" if c == DC - 1 else None)
            return (("po", pv), ("po", pv))
        P.w("scalar", "pe", pipe["xsq_pe"])
        P.w("vector", "pe", pipe["xsq_pe"])
        P.w("scalar", "gq", gv)
        P.w("vector", "gq", gv)
        sqv = cvv = None
        for c in range(DC):
            sqv = P.i("scalar", lambda e, c=c, xs=xs:
                      e.activation(out=x2b[:, c, :], in_=xs[:, c, :],
                                   func=AF.Square),
                      inc="sc" if c == DC - 1 else None)
            cvv = P.i("vector", lambda e, c=c, xs=xs:
                      e.tensor_copy(xbf[:, c, :], xs[:, c, :]),
                      inc="ve" if c == DC - 1 else None)
        return (("sc", sqv), ("ve", cvv))

    def ln_stats(prep):
        (ssem, sqv), (csem, cvv) = prep
        P.w("tensor", ssem, sqv)
        P.w("tensor", csem, cvv)
        sf = pipe["stats_free"]
        P.w("tensor", sf[0][0], sf[0][1])
        P.w("tensor", sf[1][0], sf[1][1])
        for c in range(DC):
            P.i("tensor", lambda e, c=c:
                e.matmul(pb[6], ones_bf, xbf[:, c, :],
                         start=(c == 0), stop=(c == DC - 1)))
        pe_stats = None
        for c in range(DC):
            pe_stats = P.i("tensor", lambda e, c=c:
                           e.matmul(pb[7], ones_bf, x2b[:, c, :],
                                    start=(c == 0), stop=(c == DC - 1)),
                           inc="pe" if c == DC - 1 else None)
        pipe["xsq_pe"] = pe_stats
        return pe_stats

    def ln_math_apply(s, pe_stats, lb_gate_pe, lb_store_gq=0, xs=None, lb=None):
        m_s, m2var, sda = Sm[:, 0, :], Sm[:, 1, :], Sm[:, 2, :]
        P.w("scalar", "pe", pe_stats)
        P.w("scalar", "ve", pipe["sm_ve"])
        P.i("scalar", lambda e: e.activation(out=m_s, in_=pb[6],
                                             func=AF.Copy, scale=1.0 / D))
        scm = P.i("scalar", lambda e: e.activation(out=m2var, in_=m_s,
                                                   func=AF.Square), inc="sc")
        P.w("vector", "sc", scm)
        P.w("vector", "pe", pe_stats)
        vev = P.i("vector", lambda e:
                  e.scalar_tensor_tensor(out=m2var, in0=pb[7], scalar=1.0 / D,
                                         in1=m2var, op0=ALU.mult,
                                         op1=ALU.subtract), inc="ve")
        pipe["stats_free"] = (("sc", scm), ("ve", vev))
        P.w("scalar", "ve", vev)
        P.i("scalar", lambda e: e.activation(out=sda, in_=m2var,
                                             func=AF.Ln, bias=eps5))
        scsd = P.i("scalar", lambda e: e.activation(out=sda, in_=sda,
                                                    func=AF.Exp, scale=-0.5),
                   inc="sc")
        P.w("vector", "sc", scsd)
        P.i("vector", lambda e: e.tensor_mul(m_s, m_s, sda), inc="ve")
        xs = Xs if xs is None else xs
        P.w("vector", "pe", lb_gate_pe)
        P.w("vector", "gq", lb_store_gq)
        vap = None
        for c in range(DC):
            P.i("vector", lambda e, c=c, xs=xs:
                e.tensor_mul(m2var, xs[:, c, :], sda))
            vap = P.i("vector", lambda e, c=c, s=s, lb=(Lb if lb is None else lb):
                      e.tensor_sub(lb[:, c, s * 512:(s + 1) * 512],
                                   m2var, m_s),
                      inc="ve" if c == DC - 1 else None)
        pipe["sm_ve"] = vap
        return vap

    def emit_ln(srcap, t, x_free, lb_gate_pe, lb_store_gq=0):
        g0 = ln_load(srcap, t * TT, x_free)
        p0 = ln_prep(g0)
        st0 = ln_stats(p0)
        va0 = ln_math_apply(0, st0, lb_gate_pe, lb_store_gq)
        g1 = ln_load(srcap, t * TT + 512, ("ve", va0))
        p1 = ln_prep(g1)
        st1 = ln_stats(p1)
        return ln_math_apply(1, st1, lb_gate_pe, lb_store_gq)

    def proj_pass(wname, n_oc, n_ic, rhs_slice, evac, banks3, wready_i,
                  extra_rhs_wait=None, hook=None):
        cols = n_ic * 128
        nslot = 4 if cols <= 2048 else 2
        for oc in range(n_oc):
            wpump(8)
            if hook is not None:
                hook(oc)
            slot = oc % nslot
            b = slot % 2
            off = (slot // 2) * 2048 if nslot == 4 else 0
            qsem = "sa" if b == 0 else "sb"
            P.w("sync", "wrd", wready_i, defer=True)
            if nslot == 4:
                P.w("sync", "pe", wbuf_free2[slot])
            else:
                P.w("sync", "pe", max(wbuf_free2[b], wbuf_free2[b + 2]))
            wl = P.i("sync", lambda e, oc=oc, b=b, off=off, w=wz[wname], cols=cols:
                     e.dma_start(out=Ws[:, b, off:off + cols], in_=w[oc]),
                     inc=qsem, n=16)
            bk0, bk1 = banks3[oc % len(banks3)]
            P.w("tensor", qsem, wl)
            if extra_rhs_wait is not None:
                P.w("tensor", extra_rhs_wait[0], extra_rhs_wait[1])
            P.w("tensor", bank_evac[bk0][0], bank_evac[bk0][1])
            P.w("tensor", bank_evac[bk1][0], bank_evac[bk1][1])
            pev = None
            for i in range(n_ic):
                st, sp = (i == 0), (i == n_ic - 1)
                P.i("tensor", lambda e, i=i, b=b, off=off, st=st, sp=sp, bk=bk0:
                    e.matmul(pb[bk], Ws[:, b, off + i * 128:off + (i + 1) * 128],
                             rhs_slice(i, 0), start=st, stop=sp))
                pev = P.i("tensor", lambda e, i=i, b=b, off=off, st=st, sp=sp, bk=bk1:
                          e.matmul(pb[bk], Ws[:, b, off + i * 128:off + (i + 1) * 128],
                                   rhs_slice(i, 1), start=st, stop=sp),
                          inc="pe" if sp else None)
            bank_mm[bk0] = bank_mm[bk1] = pev
            if nslot == 4:
                wbuf_free2[slot] = pev
            else:
                wbuf_free2[b] = wbuf_free2[b + 2] = pev
            bank_evac[bk0] = bank_evac[bk1] = evac(oc, bk0 // 2)

    def ffn(srcap, dst, ve_ln0, x_free, wready):
        ve_ln = ve_ln0
        store_gq = 0
        B3P = [(0, 1), (2, 3), (4, 5)]
        for t in range(NT):
            nxt = t + 1 < NT
            stats_box = {}
            prep0 = None
            if nxt:
                g0 = ln_load(srcap, (t + 1) * TT, x_free)
                prep0 = ln_prep(g0)

            def hook0(oc, prep=prep0, box=stats_box):
                if prep is not None and oc == 4:
                    box[0] = ln_stats(prep)

            def evac_w1(oc, g):
                P.w("scalar", "pe", bank_mm[2 * g])
                v = P.i("scalar", lambda e, oc=oc, g=g:
                        e.activation(out=A3[:, oc, :], in_=pbw[g], func=AF.Silu,
                                     scale=svb["s1"][:, oc:oc + 1]), inc="sc")
                return ("sc", v)
            proj_pass("w1", HC, DC,
                      lambda i, s: Lb[:, i, s * 512:(s + 1) * 512],
                      evac_w1, B3P, wready["w1"], ("ve", ve_ln), hook0)

            def evac_w2(oc, g):
                P.w("vector", "pe", bank_mm[2 * g])
                P.w("vector", "sc", P.cnt["sc"])
                v = P.i("vector", lambda e, oc=oc, g=g:
                        e.tensor_mul(A3[:, oc, :], pbw[g], A3[:, oc, :]),
                        inc="ve")
                return ("ve", v)
            proj_pass("w2", HC, DC,
                      lambda i, s: Lb[:, i, s * 512:(s + 1) * 512],
                      evac_w2, B3P, wready["w2"], ("ve", ve_ln))
            p_w2 = P.cnt["pe"]
            ve_h = P.cnt["ve"]
            prep1 = None
            if nxt:
                va0 = ln_math_apply(0, stats_box[0], p_w2)
                g1 = ln_load(srcap, (t + 1) * TT + 512, ("ve", va0))
                prep1 = ln_prep(g1)

            def hook1(oc, prep=prep1, box=stats_box):
                if prep is not None and oc == 8:
                    box[1] = ln_stats(prep)

            def evac_h1(oc, g):
                P.w("scalar", "pe", bank_mm[2 * g])
                if oc >= 14:
                    P.w("scalar", "wrd", wready["wv"], defer=True)
                v = P.i("scalar", lambda e, oc=oc, g=g:
                        e.activation(out=B3[:, oc, :], in_=pbw[g], func=AF.Gelu,
                                     scale=svb["h1s"][:, oc:oc + 1]), inc="sc")
                return ("sc", v)
            proj_pass("h1", HC, HC,
                      lambda i, s: A3[:, i, s * 512:(s + 1) * 512],
                      evac_h1, [(0, 1), (2, 3), (4, 5)], wready["h1"],
                      ("ve", ve_h), hook1)
            sc_g1 = P.cnt["sc"]
            if nxt:
                ve_ln_next = ln_math_apply(1, stats_box[1], p_w2)

            def evac_h2(oc, g):
                P.w("scalar", "pe", bank_mm[2 * g])
                v = P.i("scalar", lambda e, oc=oc, g=g:
                        e.activation(out=A3[:, oc, :], in_=pbw[g], func=AF.Gelu,
                                     scale=svb["h2s"][:, oc:oc + 1]), inc="sc")
                return ("sc", v)
            proj_pass("h2", HC, HC,
                      lambda i, s: B3[:, i, s * 512:(s + 1) * 512],
                      evac_h2, [(0, 1), (2, 3), (4, 5)], wready["h2"],
                      ("sc", sc_g1))
            sc_h2 = P.cnt["sc"]

            p_h2 = P.cnt["pe"]
            for k in range(4):
                w3ring_free[k] = max(w3ring_free[k], p_h2)
            w3ring = [B3[:, 4 * k:4 * k + 4, :].rearrange("p a b -> p (a b)")
                      for k in range(4)]
            xg = ("ve", pipe["sm_ve"])
            for s in range(2):
                tc = t * TT + s * 512
                P.w("gpsimd", xg[0], xg[1])
                P.w("gpsimd", "gq", store_gq)
                ld = P.i("gpsimd", lambda e, tc=tc, sr=srcap:
                         e.dma_start(out=Xs, in_=sr[:, tc:tc + 512].rearrange(
                             "(c p) t -> p c t", p=128)), inc="gq", n=16)
                vv = None
                for oc in range(DC):
                    wpump(8)
                    slot = (s * DC + oc) % 4
                    qsem = "sa" if slot % 2 == 0 else "sb"
                    P.w("sync", "wrd", wready["w3"], defer=True)
                    P.w("sync", "pe", w3ring_free[slot])
                    wl = P.i("sync", lambda e, oc=oc, slot=slot:
                             e.dma_start(out=w3ring[slot], in_=wz["w3"][oc]),
                             inc=qsem, n=16)
                    bk = oc % 4
                    P.w("tensor", qsem, wl)
                    P.w("tensor", "sc", sc_h2)
                    P.w("tensor", bank_evac[bk][0], bank_evac[bk][1])
                    pev = None
                    for i in range(HC):
                        pev = P.i("tensor", lambda e, i=i, slot=slot, s=s, bk=bk:
                                  e.matmul(pb[bk],
                                           w3ring[slot][:, i * 128:(i + 1) * 128],
                                           A3[:, i, s * 512:(s + 1) * 512],
                                           start=(i == 0), stop=(i == HC - 1)),
                                  inc="pe" if i == HC - 1 else None)
                    bank_mm[bk] = pev
                    w3ring_free[slot] = pev
                    P.w("vector", "pe", pev)
                    P.w("vector", "gq", ld)
                    vv = P.i("vector", lambda e, oc=oc, bk=bk:
                             e.scalar_tensor_tensor(out=Xs[:, oc, :], in0=pb[bk],
                                                    scalar=halfs3[:, oc:oc + 1],
                                                    in1=Xs[:, oc, :],
                                                    op0=ALU.mult, op1=ALU.add),
                             inc="ve")
                    bank_evac[bk] = ("ve", vv)
                P.w("gpsimd", "ve", vv)
                store_gq = P.i("gpsimd", lambda e, tc=tc:
                               e.dma_start(out=dst[:, tc:tc + 512].rearrange(
                                               "(c p) t -> p c t", p=128),
                                           in_=Xs), inc="gq", n=16)
                xg = ("gq", store_gq)
            x_free = xg
            if nxt:
                ve_ln = ve_ln_next
        return {"store_gq": store_gq, "x_free": x_free}

    init_consts()
    cur_src = xT
    x_free = ("pe", 0)
    ve_ln0 = emit_ln(xT, 0, x_free, 0) if "f1" in phases else None

    if "f1" in phases:
        wpump(128)
        dst1 = x1_scr if ("b" in phases or "f2" in phases) else outT
        r = ffn(xT, dst1, ve_ln0, ("ve", ve_ln0), wready)
        x_free = r["x_free"]
        cur_src = dst1

    wpump(len(WITEMS))
    if "b" in phases:
        b_src = cur_src
        P.w("sync", "wrd", wready["wv"])
        P.w("sync", "pe", P.cnt["pe"])
        wkl = None
        for c in range(DC):
            P.i("sync", lambda e, c=c:
                e.dma_start(out=B3[:, 16 + c, :],
                            in_=wp["wk"][c * 128:(c + 1) * 128, :]), inc="sa", n=16)
            wkl = P.i("sync", lambda e, c=c:
                      e.dma_start(out=B3[:, 24 + c, :],
                                  in_=wp["wv"][c * 128:(c + 1) * 128, :]), inc="sa", n=16)
        P.w("vector", "wrd", wready["wv"])
        P.i("vector", lambda e: e.memset(vstage, 0.0), inc="ve")
        kvadd = 0
        ve_ln = emit_ln(b_src, 0, x_free, P.cnt["pe"])
        P.w("gpsimd", "ve", ve_ln)
        lnst = P.i("gpsimd", lambda e:
                   e.dma_start(out=ln_scr[:, 0:TT].rearrange(
                                   "(c p) t -> p c t", p=128),
                               in_=Lb), inc="gq", n=16)
        xs2b1 = A3.rearrange("p a b -> p (a b)")[:, 0:8192].bitcast(F32)\
            .rearrange("p (c f) -> p c f", c=DC)
        lnbufs = [Lb, A3[:, 8:16, :]]
        xs2_ve = 0
        xs2_pe0 = P.cnt["pe"]
        prev_pkv7 = xs2_pe0
        for t in range(NT):
            lnb_cur = lnbufs[t % 2]
            lnb_nxt = lnbufs[(t + 1) % 2]
            nxt = t + 1 < NT
            bbox = {}
            if nxt:
                g0 = ln_load(b_src, (t + 1) * TT, ("ve", pipe["sm_ve"]))
                if t == 0:
                    P.w("gpsimd", "pe", xs2_pe0)
                g1 = ln_load(b_src, (t + 1) * TT + 512, ("ve", xs2_ve), xs2b1)
                bbox["p0"] = ln_prep(g0, pool=True)
            k1 = v1 = None
            pkv_tl = [None] * 8
            for tl in range(8):
                if nxt and tl == 2:
                    bbox[0] = ln_stats(bbox["p0"])
                if nxt and tl == 4:
                    va0 = ln_math_apply(0, bbox[0], prev_pkv7, lnst, lb=lnb_nxt)
                    bbox["p1"] = ln_prep(g1, xs2b1, pool=True)
                if nxt and tl == 6:
                    bbox[1] = ln_stats(bbox["p1"])
                P.w("tensor", "sa", wkl)
                P.w("tensor", "ve", ve_ln)
                bko = 0
                for bk in range(bko, bko + 4):
                    P.w("tensor", bank_evac[bk][0], bank_evac[bk][1])
                pkv = None
                for i in range(DC):
                    st, sp = (i == 0), (i == DC - 1)
                    P.i("tensor", lambda e, i=i, tl=tl, st=st, sp=sp, L=lnb_cur, bk=bko:
                        e.matmul(pb[bk], L[:, i, tl * 128:(tl + 1) * 128],
                                 B3[:, 16 + i, 0:512], start=st, stop=sp))
                    P.i("tensor", lambda e, i=i, tl=tl, st=st, sp=sp, L=lnb_cur, bk=bko + 1:
                        e.matmul(pb[bk], L[:, i, tl * 128:(tl + 1) * 128],
                                 B3[:, 16 + i, 512:1024], start=st, stop=sp))
                    P.i("tensor", lambda e, i=i, tl=tl, st=st, sp=sp, L=lnb_cur, bk=bko + 2:
                        e.matmul(pb[bk], L[:, i, tl * 128:(tl + 1) * 128],
                                 B3[:, 24 + i, 0:512], start=st, stop=sp))
                    pkv = P.i("tensor", lambda e, i=i, tl=tl, st=st, sp=sp, L=lnb_cur, bk=bko + 3:
                              e.matmul(pb[bk], L[:, i, tl * 128:(tl + 1) * 128],
                                       B3[:, 24 + i, 512:1024], start=st, stop=sp),
                              inc="pe" if sp else None)
                pkv_tl[tl] = pkv
                for bk in range(bko, bko + 4):
                    bank_mm[bk] = pkv
                P.w("scalar", "pe", pkv)
                k1 = P.i("scalar", lambda e, tl=tl:
                         e.activation(out=B3[:, tl, :], in_=pbw[0],
                                      func=AF.Relu, scale=0.125), inc="sc")
                P.w("vector", "pe", pkv)
                v1 = P.i("vector", lambda e, tl=tl:
                         e.tensor_copy(B3[:, 8 + tl, :], pbw[1]), inc="ve")
                bank_evac[0] = bank_evac[1] = ("sc", k1)
                bank_evac[2] = bank_evac[3] = ("ve", v1)
            if nxt:
                ve_ln_n = ln_math_apply(1, bbox[1], prev_pkv7, lnst, xs2b1,
                                        lb=lnb_nxt)
            P.w("tensor", "sc", k1)
            P.w("tensor", "ve", v1)
            P.w("tensor", bank_evac[4][0], bank_evac[4][1])
            P.w("tensor", bank_evac[5][0], bank_evac[5][1])
            pkvm = None
            for h in range(16):
                for tl in range(8):
                    bk = 4 + h // 8
                    off = (h % 8) * 64
                    pkvm = P.i("tensor",
                               lambda e, tl=tl, h=h, bk=bk, off=off:
                               e.matmul(pb[bk][0:64, off:off + 64],
                                        B3[:, tl, h * 64:(h + 1) * 64],
                                        B3[:, 8 + tl, h * 64:(h + 1) * 64],
                                        start=(tl == 0), stop=(tl == 7),
                                        skip_group_check=True),
                               inc="pe" if (tl == 7 and h == 15) else None)
            bank_mm[4] = bank_mm[5] = pkvm
            P.w("vector", "pe", pkvm)
            P.i("vector", lambda e: e.tensor_add(vstage[0:64, 0:512],
                                                 vstage[0:64, 0:512],
                                                 pb[4][0:64, :]))
            kvadd = P.i("vector", lambda e: e.tensor_add(vstage[0:64, 512:1024],
                                                         vstage[0:64, 512:1024],
                                                         pb[5][0:64, :]), inc="ve")
            bank_evac[4] = bank_evac[5] = ("ve", kvadd)
            if nxt:
                ve_ln = ve_ln_n
                xs2_ve = ve_ln
                P.w("gpsimd", "ve", ve_ln)
                lnst = P.i("gpsimd", lambda e, t=t, L=lnb_nxt:
                           e.dma_start(
                               out=ln_scr[:, (t + 1) * TT:(t + 2) * TT].rearrange(
                                   "(c p) t -> p c t", p=128),
                               in_=L), inc="gq", n=16)
            prev_pkv7 = pkv_tl[7]
        x_free = ("ve", pipe["sm_ve"])
        wbuf_free2[0] = wbuf_free2[1] = wbuf_free2[2] = wbuf_free2[3] = pkvm
        P.w("vector", "ve", kvadd)
        P.i("vector", lambda e: e.tensor_copy(vout[0:64, :], vstage[0:64, :]))
        P.i("vector", lambda e: e.memset(kv_bd, 0.0), inc="ve")
        for j in range(8):
            P.i("vector", lambda e, j=j:
                e.tensor_copy(kv_bd[0:64, j * 128:j * 128 + 64],
                              vout[0:64, 2 * j * 64:(2 * j + 1) * 64]),
                inc="ve" if j == 7 else None)
        kfve = P.cnt["ve"]
        P.w("gpsimd", "ve", kfve)
        kfg = None
        for j in range(8):
            kfg = P.i("gpsimd", lambda e, j=j:
                      e.dma_start(out=kv_bd[64:128, j * 128 + 64:(j + 1) * 128],
                                  in_=vout[0:64, (2 * j + 1) * 64:(2 * j + 2) * 64]),
                      inc="gq", n=16)
        dst2 = x2_scr if "f2" in phases else outT
        store_gq2 = 0
        lnb2 = [Lb, B3[:, 8:16, :]]
        qu_pe_hist = [0, 0]
        for t in range(NT):
            Lc = lnb2[t % 2]
            P.w("gpsimd", "gq", kfg)
            P.w("gpsimd", "pe", qu_pe_hist[t % 2])
            lnl = []
            for h in range(4):
                lnl.append(P.i("gpsimd", lambda e, t=t, h=h, L=Lc:
                               e.dma_start(out=L[:, 2 * h:2 * h + 2, :],
                                           in_=ln_scr[h * 256:(h + 1) * 256,
                                                      t * TT:(t + 1) * TT]
                                           .rearrange("(c p) t -> p c t", p=128)),
                               inc="gq", n=16))
            quev = None
            for oc in range(DC):
                slot = oc % 4
                b = slot % 2
                off = (slot // 2) * 2048
                qsem = "sa" if b == 0 else "sb"
                P.w("sync", "wrd", wready["wu"])
                P.w("sync", "pe", wbuf_free2[slot])
                P.i("sync", lambda e, oc=oc, b=b, off=off:
                    e.dma_start(out=Ws[:, b, off:off + 1024], in_=wz["wq"][oc]),
                    inc=qsem, n=16)
                wl = P.i("sync", lambda e, oc=oc, b=b, off=off:
                         e.dma_start(out=Ws[:, b, off + 1024:off + 2048],
                                     in_=wz["wu"][oc]), inc=qsem, n=16)
                bks = [4 * (oc % 2) + k for k in range(4)]
                P.w("tensor", qsem, wl)
                P.w("tensor", "gq", lnl[-1])
                for bk in bks:
                    P.w("tensor", bank_evac[bk][0], bank_evac[bk][1])
                ppe = None
                for i in range(DC):
                    st, sp = (i == 0), (i == DC - 1)
                    P.i("tensor", lambda e, i=i, b=b, o=off, st=st, sp=sp, bk=bks[0], L=Lc:
                        e.matmul(pb[bk], Ws[:, b, o + i * 128:o + (i + 1) * 128],
                                 L[:, i, 0:512], start=st, stop=sp))
                    P.i("tensor", lambda e, i=i, b=b, o=off, st=st, sp=sp, bk=bks[1], L=Lc:
                        e.matmul(pb[bk], Ws[:, b, o + i * 128:o + (i + 1) * 128],
                                 L[:, i, 512:1024], start=st, stop=sp))
                    P.i("tensor", lambda e, i=i, b=b, o=off, st=st, sp=sp, bk=bks[2], L=Lc:
                        e.matmul(pb[bk],
                                 Ws[:, b, o + 1024 + i * 128:o + 1024 + (i + 1) * 128],
                                 L[:, i, 0:512], start=st, stop=sp))
                    ppe = P.i("tensor", lambda e, i=i, b=b, o=off, st=st, sp=sp, bk=bks[3], L=Lc:
                              e.matmul(pb[bk],
                                       Ws[:, b, o + 1024 + i * 128:o + 1024 + (i + 1) * 128],
                                       L[:, i, 512:1024], start=st, stop=sp),
                              inc="pe" if sp else None)
                for bk in bks:
                    bank_mm[bk] = ppe
                wbuf_free2[slot] = ppe
                P.w("scalar", "pe", ppe)
                P.i("scalar", lambda e, oc=oc, bk=bks[0]:
                    e.activation(out=A3[:, oc, 0:512], in_=pb[bk],
                                 func=AF.Relu, scale=0.125))
                P.i("scalar", lambda e, oc=oc, bk=bks[1]:
                    e.activation(out=A3[:, oc, 512:1024], in_=pb[bk],
                                 func=AF.Relu, scale=0.125))
                P.i("scalar", lambda e, oc=oc, bk=bks[2]:
                    e.activation(out=A3[:, 8 + oc, 0:512], in_=pb[bk],
                                 func=AF.Silu))
                quev = P.i("scalar", lambda e, oc=oc, bk=bks[3]:
                           e.activation(out=A3[:, 8 + oc, 512:1024], in_=pb[bk],
                                        func=AF.Silu), inc="sc")
                for bk in bks:
                    bank_evac[bk] = ("sc", quev)
            qu_pe_hist[t % 2] = P.cnt["pe"]
            P.w("tensor", "gq", kfg)
            x2f = x2b.rearrange("p a b -> p (a b)")
            smf = Sm.rearrange("p a b -> p (a b)")
            xsf = Xs.rearrange("p a b -> p (a b)")
            SETS = [
                {"a2": x2f[:, 0:1024], "sd": smf[:, 512:1536],
                 "t1": vstage[:, 0:1024]},
                {"a2": x2f[:, 1024:2048], "sd": xsf[:, 0:1024],
                 "t1": xsf[:, 1024:2048]},
            ]
            prev_n2 = [("pe", 0), ("pe", 0)]
            prev_anu = [0, 0]
            anu = None
            for j in range(8):
                st_ = SETS[j % 2]
                aw = pbw[j % 2]
                nw = pbw[2 + j % 2]
                P.w("tensor", "sc", quev)
                P.w("tensor", bank_evac[2 * (j % 2)][0], bank_evac[2 * (j % 2)][1])
                P.w("tensor", bank_evac[2 * (j % 2) + 1][0],
                    bank_evac[2 * (j % 2) + 1][1])
                P.i("tensor", lambda e, j=j, aw=aw:
                    e.matmul(aw[:, 0:512], kv_bd[:, j * 128:(j + 1) * 128],
                             A3[:, j, 0:512], start=True, stop=True))
                att_pe = P.i("tensor", lambda e, j=j, aw=aw:
                             e.matmul(aw[:, 512:1024],
                                      kv_bd[:, j * 128:(j + 1) * 128],
                                      A3[:, j, 512:1024],
                                      start=True, stop=True), inc="pe")
                bank_mm[2 * (j % 2)] = bank_mm[2 * (j % 2) + 1] = att_pe
                P.w("scalar", "pe", att_pe)
                P.w("scalar", prev_n2[j % 2][0], prev_n2[j % 2][1])
                sqa = P.i("scalar", lambda e, aw=aw, o=st_["a2"]:
                          e.activation(out=o, in_=aw, func=AF.Square), inc="sc")
                P.w("tensor", "sc", sqa)
                P.w("tensor", bank_evac[4 + 2 * (j % 2)][0],
                    bank_evac[4 + 2 * (j % 2)][1])
                P.w("tensor", bank_evac[5 + 2 * (j % 2) - (j % 2)][0] if False else
                    bank_evac[4 + 2 * (j % 2) + 1][0],
                    bank_evac[4 + 2 * (j % 2) + 1][1])
                P.i("tensor", lambda e, nw=nw, i_=st_["a2"]:
                    e.matmul(nw[:, 0:512], bdo, i_[:, 0:512],
                             start=True, stop=True))
                n2pe = P.i("tensor", lambda e, nw=nw, i_=st_["a2"]:
                           e.matmul(nw[:, 512:1024], bdo, i_[:, 512:1024],
                                    start=True, stop=True), inc="pe")
                bank_mm[4 + 2 * (j % 2)] = bank_mm[4 + 2 * (j % 2) + 1] = n2pe
                prev_n2[j % 2] = ("pe", n2pe)
                P.w("scalar", "pe", n2pe)
                P.w("scalar", "ve", prev_anu[j % 2])
                P.i("scalar", lambda e, nw=nw, o=st_["sd"]:
                    e.activation(out=o, in_=nw, func=AF.Ln,
                                 scale=1.0 / 64, bias=eps24))
                sda2 = P.i("scalar", lambda e, o=st_["sd"]:
                           e.activation(out=o, in_=o, func=AF.Exp, scale=-0.5),
                           inc="sc")
                bank_evac[4 + 2 * (j % 2)] = ("sc", sda2)
                bank_evac[4 + 2 * (j % 2) + 1] = ("sc", sda2)
                P.w("vector", "sc", sda2)
                P.i("vector", lambda e, aw=aw, o=st_["t1"], c=st_["sd"]:
                    e.tensor_mul(o, aw, c))
                anu = P.i("vector", lambda e, j=j, i_=st_["t1"]:
                          e.tensor_mul(A3[:, 16 + j, :], i_, A3[:, 8 + j, :]),
                          inc="ve")
                prev_anu[j % 2] = anu
                bank_evac[2 * (j % 2)] = ("ve", anu)
                bank_evac[2 * (j % 2) + 1] = ("ve", anu)
            anu_all = anu
            for s in range(2):
                tc = t * TT + s * 512
                xf = ("gq", store_gq2) if store_gq2 else x_free
                P.w("gpsimd", xf[0], xf[1])
                P.w("gpsimd", "ve", anu_all)
                lds = [P.i("gpsimd", lambda e, tc=tc:
                           e.dma_start(out=Xs,
                                       in_=b_src[:, tc:tc + 512].rearrange(
                                           "(c p) t -> p c t", p=128)),
                       inc="gq", n=16)]
                for oc in range(DC):
                    slot = (s * DC + oc) % 4
                    b = slot % 2
                    off = (slot // 2) * 2048
                    qsem = "sa" if b == 0 else "sb"
                    P.w("sync", "wrd", wready["wo"])
                    P.w("sync", "pe", wbuf_free2[slot])
                    wl = P.i("sync", lambda e, oc=oc, b=b, o=off:
                             e.dma_start(out=Ws[:, b, o:o + 1024], in_=wz["wo"][oc]),
                             inc=qsem, n=16)
                    bk = oc % 4
                    P.w("tensor", qsem, wl)
                    P.w("tensor", "ve", anu_all)
                    P.w("tensor", bank_evac[bk][0], bank_evac[bk][1])
                    pev = None
                    for i in range(DC):
                        pev = P.i("tensor", lambda e, i=i, b=b, o=off, s=s, bk=bk:
                                  e.matmul(pb[bk], Ws[:, b, o + i * 128:o + (i + 1) * 128],
                                           A3[:, 16 + i, s * 512:(s + 1) * 512],
                                           start=(i == 0), stop=(i == DC - 1)),
                                  inc="pe" if i == DC - 1 else None)
                    bank_mm[bk] = pev
                    wbuf_free2[slot] = pev
                    P.w("vector", "pe", pev)
                    P.w("vector", "gq", lds[-1])
                    vv = P.i("vector", lambda e, oc=oc, bk=bk:
                             e.tensor_add(Xs[:, oc, :], pb[bk], Xs[:, oc, :]),
                             inc="ve")
                    bank_evac[bk] = ("ve", vv)
                P.w("gpsimd", "ve", vv)
                store_gq2 = P.i("gpsimd", lambda e, tc=tc:
                                e.dma_start(out=dst2[:, tc:tc + 512].rearrange(
                                                "(c p) t -> p c t", p=128),
                                            in_=Xs), inc="gq", n=16)
        x_free = ("gq", store_gq2)
        cur_src = dst2

    if "f2" in phases:
        pipe["xsq_pe"] = P.cnt["pe"]
        pipe["stats_free"] = (("sc", P.cnt["sc"]), ("ve", P.cnt["ve"]))
        ve_ln0b = emit_ln(cur_src, 0, x_free, P.cnt["pe"])
        ffn(cur_src, outT, ve_ln0b, ("ve", ve_ln0b), wready)

    if dbg:
        BF = BF16
        dbg_ln = nc.dram_tensor("dbg_ln", [D, L], BF, kind="ExternalOutput").ap()
        dbg_x1 = nc.dram_tensor("dbg_x1", [D, L], F32, kind="ExternalOutput").ap()
        dbg_a3 = nc.dram_tensor("dbg_a3", [128, 32, 1024], BF, kind="ExternalOutput").ap()
        dbg_kv = nc.dram_tensor("dbg_kv", [128, 1024], BF, kind="ExternalOutput").ap()
        P.w("gpsimd", "gq", P.cnt["gq"])
        P.w("gpsimd", "pe", P.cnt["pe"])
        P.w("gpsimd", "ve", P.cnt["ve"])
        P.w("gpsimd", "sc", P.cnt["sc"])
        P.i("gpsimd", lambda e: e.dma_start(out=dbg_ln, in_=ln_scr), inc="gq", n=16)
        P.i("gpsimd", lambda e: e.dma_start(out=dbg_x1, in_=x1_scr), inc="gq", n=16)
        P.i("gpsimd", lambda e: e.dma_start(out=dbg_a3, in_=A3), inc="gq", n=16)
        P.i("gpsimd", lambda e: e.dma_start(out=dbg_kv, in_=kv_bd), inc="gq", n=16)
    P.w("gpsimd", "gq", P.cnt["gq"])
    P.w("gpsimd", "sa", P.cnt["sa"])
    P.w("gpsimd", "sb", P.cnt["sb"])
    for _k in range(12):
        P.w("gpsimd", f"ws{_k}", P.cnt[f"ws{_k}"])
    P.w("sync", "sa", P.cnt["sa"])
    P.w("sync", "sb", P.cnt["sb"])

    with contextlib.ExitStack() as stack:
        sh = {s: stack.enter_context(nc.semaphore(s)) for s in SEMS}
        block = stack.enter_context(nc.Block())

        def runner(name):
            def _(eng):
                for item in P.stream[name]:
                    if item[0] == "wait":
                        eng.wait_ge(sh[item[1]], item[2])
                    else:
                        _k, fn, sem, n = item
                        ins = fn(eng)
                        if sem is not None:
                            ins.then_inc(sh[sem], n)
            return _
        block.sync(runner("sync"))
        block.scalar(runner("scalar"))
        block.vector(runner("vector"))
        block.tensor(runner("tensor"))
        block.gpsimd(runner("gpsimd"))
    return nc


def _prep_inputs(inp):
    T = lambda a: np.ascontiguousarray(np.asarray(a).T)
    g = {
        "wq_t": T(inp["L0_attn_Wq"]), "wk_t": T(inp["L0_attn_Wk"]),
        "wv_t": T(inp["L0_attn_Wv"]), "wu_t": T(inp["L0_attn_Wu"]),
        "wo_t": T(inp["L0_attn_Wo"]), "w1_t": T(inp["L0_ffn_w1_w"]),
        "w2_t": T(inp["L0_ffn_w2_w"]), "h1_t": T(inp["L0_ffn_h1_w"]),
        "h2_t": T(inp["L0_ffn_h2_w"]), "w3_t": T(inp["L0_ffn_w3_w"]),
        "s1_r": T(np.asarray(inp["L0_ffn_w1_s"]).reshape(HC, 128)),
        "s2_r": T(np.asarray(inp["L0_ffn_w2_s"]).reshape(HC, 128)),
        "h1s_r": T(np.asarray(inp["L0_ffn_h1_s"]).reshape(HC, 128)),
        "h2s_r": T(np.asarray(inp["L0_ffn_h2_s"]).reshape(HC, 128)),
        "w3s_r": T(np.asarray(inp["L0_ffn_w3_s"]).reshape(DC, 128)),
    }
    return g


def kernel(**inp):
    x = np.asarray(inp["L0_attn_x"])
    assert x.shape == (NCORES, L, D)
    g = _prep_inputs(inp)
    nc = build()
    in_maps = []
    for c in range(NCORES):
        m = dict(g)
        m["xT"] = np.ascontiguousarray(x[c].T)
        in_maps.append(m)
    res = run_bass_kernel_spmd(nc, in_maps, core_ids=list(range(NCORES)))
    out = np.stack([np.asarray(res.results[c]["outT"]).T for c in range(NCORES)])
    return np.ascontiguousarray(out.astype(np.float32))


# version 18
# speedup vs baseline: 1.0834x; 1.0834x over previous
import contextlib
import numpy as np
import concourse.bass as bass
import concourse.mybir as mybir
from concourse.bass_utils import run_bass_kernel_spmd

F32 = mybir.dt.float32
BF16 = mybir.dt.bfloat16
AF = mybir.ActivationFunctionType
ALU = mybir.AluOpType

L, D, H = 8192, 1024, 4096
TT = 1024
NT = L // TT
DC, HC = D // 128, H // 128
NCORES = 8

ENGS = ("sync", "scalar", "vector", "tensor", "gpsimd")
SEMS = tuple(["sa", "sb", "gq", "pe", "sc", "ve", "po", "wrd"]
             + [f"ws{k}" for k in range(12)] + [f"gs{k}" for k in range(8)])


class Prog:
    def __init__(self):
        self.stream = {e: [] for e in ENGS}
        self.cnt = {s: 0 for s in SEMS}

    def w(self, eng, sem, val, defer=False):
        if val is None or val <= 0:
            return
        assert defer or val <= self.cnt[sem], (
            f"wait {sem}>={val} on {eng} but only {self.cnt[sem]} planned")
        st = self.stream[eng]
        if st and st[-1][0] == "wait" and st[-1][1] == sem:
            if st[-1][2] >= val:
                return
            st[-1] = ("wait", sem, val)
            return
        st.append(("wait", sem, val))

    def i(self, eng, fn, inc=None, n=1):
        if inc is None:
            self.stream[eng].append(("op", fn, None, 0))
            return None
        self.cnt[inc] += n
        self.stream[eng].append(("op", fn, inc, n))
        return self.cnt[inc]


def build(phases=("f1", "b", "f2"), dbg=False):
    nc = bass.Bass("TRN2", target_bir_lowering=False)
    P = Prog()

    xT = nc.dram_tensor("xT", [D, L], F32, kind="ExternalInput").ap()
    outT = nc.dram_tensor("outT", [D, L], F32, kind="ExternalOutput").ap()
    WSPEC = [("wq", D, D), ("wk", D, D), ("wv", D, D), ("wu", D, D),
             ("wo", D, D), ("w1", D, H), ("w2", D, H), ("h1", H, H),
             ("h2", H, H), ("w3", H, D)]
    wt = {nm: nc.dram_tensor(nm + "_t", [I, O], F32, kind="ExternalInput").ap()
          for nm, I, O in WSPEC}
    SVSPEC = [("s1", HC), ("s2", HC), ("h1s", HC), ("h2s", HC), ("w3s", DC)]
    sv_in = {nm: nc.dram_tensor(nm + "_r", [128, n], F32, kind="ExternalInput").ap()
             for nm, n in SVSPEC}
    SWZ = ("w1", "w2", "h1", "h2", "w3", "wq", "wu", "wo")
    wz = {nm: nc.dram_tensor(nm + "_z", [O // 128, 128, I // 128, 128], BF16,
                             kind="Internal").ap()
          for nm, I, O in WSPEC if nm in SWZ}
    wp = {nm: nc.dram_tensor(nm + "_p", [D, D], BF16, kind="Internal").ap()
          for nm in ("wk", "wv")}
    x1_scr = nc.dram_tensor("x1_scr", [D, L], F32, kind="Internal").ap()
    x2_scr = nc.dram_tensor("x2_scr", [D, L], F32, kind="Internal").ap()
    ln_scr = nc.dram_tensor("ln_scr", [D, L], BF16, kind="Internal").ap()

    al = lambda nm, sh, dt: nc.alloc_sbuf_tensor(nm, sh, dt).ap()
    A3 = al("A3", [128, 32, 1024], BF16)
    B3 = al("B3", [128, 32, 1024], BF16)
    Xs = al("Xs", [128, 8, 512], F32)
    Lb = al("Lb", [128, 8, 1024], BF16)
    xbf = al("xbf", [128, 8, 512], BF16)
    x2b = al("x2b", [128, 8, 512], BF16)
    Ws = al("Ws", [128, 2, 4096], BF16)
    Sm = al("Sm", [128, 3, 512], F32)
    ones_bf = al("ones_bf", [128, 128], BF16)
    bdo = al("bdo", [128, 128], BF16)
    eps5 = al("eps5", [128, 1], F32)
    eps24 = al("eps24", [128, 1], F32)
    svb = {nm: al("sv_" + nm, [128, n], F32) for nm, n in SVSPEC}
    halfs3 = al("halfs3", [128, DC], F32)
    fscr = al("fscr", [128, 512], F32)
    vstage = al("vstage", [128, 1024], F32)
    vout = al("vout", [128, 1024], BF16)
    kv_bd = xbf.rearrange("p a b -> p (a b)")[:, 0:1024]
    att2 = x2b.rearrange("p a b -> p (a b)")[:, 0:512]

    pbw = [nc.alloc_psum_tensor(f"pbw{k}", [128, 1024], F32).ap() for k in range(4)]
    pb = [pbw[k // 2][:, (k % 2) * 512:(k % 2 + 1) * 512] for k in range(8)]

    bank_evac = [("pe", 0)] * 8
    bank_mm = [0] * 8
    wbuf_free2 = [0, 0, 0, 0]
    w3ring_free = [0, 0, 0, 0]

    def init_consts():
        P.i("vector", lambda e: e.memset(ones_bf, 1.0))
        P.i("vector", lambda e: e.memset(bdo[0:64, 0:64], 1.0))
        P.i("vector", lambda e: e.memset(bdo[0:64, 64:128], 0.0))
        P.i("vector", lambda e: e.memset(bdo[64:128, 0:64], 0.0))
        P.i("vector", lambda e: e.memset(bdo[64:128, 64:128], 1.0))
        P.i("vector", lambda e: e.memset(eps5, 1e-5))
        P.i("vector", lambda e: e.memset(eps24, 1e-24), inc="ve")
        for nm, _n in SVSPEC:
            P.i("gpsimd", lambda e, a=svb[nm], b=sv_in[nm]:
                e.dma_start(out=a, in_=b), inc="gq", n=16)
        svg = P.cnt["gq"]
        P.w("scalar", "gq", svg)
        P.i("scalar", lambda e: e.activation(out=halfs3, in_=svb["w3s"],
                                             func=AF.Copy, scale=0.5), inc="sc")
        P.w("vector", "gq", svg)
        P.w("gpsimd", "gq", svg)

    W_ORDER = ["w1", "w2", "h1", "h2", "w3", "wq", "wu", "wo", "wk", "wv"]
    _wdims = {nm: (I, O) for nm, I, O in WSPEC}
    WITEMS = []
    for _nm in W_ORDER:
        _I, _O = _wdims[_nm]
        for _ic in range(_I // 128):
            for _ob in range(_O // 512):
                WITEMS.append((_nm, _ic, _ob,
                               _ic == _I // 128 - 1 and _ob == _O // 512 - 1))
    wst = {"pos": 0, "cve": [None] * 12, "t": 0}
    wvin = [B3[:, 14 + k, :].bitcast(F32) for k in range(12)]
    wvo = [B3[:, 26 + k // 2, (k % 2) * 512:(k % 2 + 1) * 512] for k in range(8)]
    wready = {nm: i + 1 for i, nm in enumerate(W_ORDER)}

    def wpump(k):
        while k > 0 and wst["pos"] < len(WITEMS):
            nm, ic, ob, last = WITEMS[wst["pos"]]
            t = wst["t"]
            si, so = t % 12, t % 8
            wsem, gsem = f"ws{si}", f"gs{so}"
            if wst["cve"][si] is not None:
                P.w("sync", wst["cve"][si][0], wst["cve"][si][1])
            srcw = wt[nm][ic * 128:(ic + 1) * 128, ob * 512:(ob + 1) * 512]
            ld = P.i("sync", lambda e, o=wvin[si], s=srcw:
                     e.dma_start(out=o, in_=s), inc=wsem, n=16)
            P.w("vector", wsem, ld)
            P.w("vector", gsem, P.cnt[gsem])
            if nm == "h1":
                cv = P.i("vector", lambda e, o=wvo[so], s=wvin[si], c=ic:
                         e.tensor_scalar(out=o, in0=s,
                                         scalar1=svb["s2"][:, c:c + 1],
                                         scalar2=None, op0=ALU.mult), inc="ve")
            else:
                cv = P.i("vector", lambda e, o=wvo[so], s=wvin[si]:
                         e.tensor_copy(o, s), inc="ve")
            wst["cve"][si] = ("ve", cv)
            P.w("gpsimd", "ve", cv)
            if nm in SWZ:
                dst = wz[nm][ob * 4:(ob + 1) * 4, :, ic, :].transpose([1, 0, 2])
                ssrc = wvo[so].rearrange("p (f1 f2) -> p f1 f2", f1=4)
            else:
                dst = wp[nm][ic * 128:(ic + 1) * 128, ob * 512:(ob + 1) * 512]
                ssrc = wvo[so]
            P.i("gpsimd", lambda e, o=dst, s=ssrc:
                e.dma_start(out=o, in_=s), inc=gsem, n=16)
            if last:
                for so2 in range(8):
                    P.w("gpsimd", f"gs{so2}", P.cnt[f"gs{so2}"])
                P.i("gpsimd", lambda e: e.engine_nop(), inc="wrd")
            wst["pos"] += 1
            wst["t"] += 1
            k -= 1

    pipe = {"stats_free": (("pe", 0), ("pe", 0)), "xsq_pe": 0, "sm_ve": 0,
            "fscr_ve": 0}

    def ln_load(srcap, tc, gate, xs=None):
        xs = Xs if xs is None else xs
        P.w("gpsimd", gate[0], gate[1])
        g = None
        for h in range(4):
            g = P.i("gpsimd", lambda e, tc=tc, h=h, s=srcap, xs=xs:
                    e.dma_start(out=xs[:, 2 * h:2 * h + 2, :],
                                in_=s[h * 256:(h + 1) * 256,
                                      tc:tc + 512].rearrange(
                                    "(c p) t -> p c t", p=128)), inc="gq", n=16)
        return g

    def ln_prep(gv, xs=None, pool=False):
        xs = Xs if xs is None else xs
        if pool:
            P.w("gpsimd", "pe", pipe["xsq_pe"])
            P.w("gpsimd", "gq", gv)
            pv = None
            for c in range(DC):
                P.i("gpsimd", lambda e, c=c, xs=xs:
                    e.tensor_mul(x2b[:, c, :], xs[:, c, :], xs[:, c, :]))
                pv = P.i("gpsimd", lambda e, c=c, xs=xs:
                         e.tensor_copy(xbf[:, c, :], xs[:, c, :]),
                         inc="po" if c == DC - 1 else None)
            return (("po", pv), ("po", pv))
        P.w("scalar", "pe", pipe["xsq_pe"])
        P.w("vector", "pe", pipe["xsq_pe"])
        P.w("scalar", "gq", gv)
        P.w("vector", "gq", gv)
        sqv = cvv = None
        for c in range(DC):
            sqv = P.i("scalar", lambda e, c=c, xs=xs:
                      e.activation(out=x2b[:, c, :], in_=xs[:, c, :],
                                   func=AF.Square),
                      inc="sc" if c == DC - 1 else None)
            cvv = P.i("vector", lambda e, c=c, xs=xs:
                      e.tensor_copy(xbf[:, c, :], xs[:, c, :]),
                      inc="ve" if c == DC - 1 else None)
        return (("sc", sqv), ("ve", cvv))

    def ln_stats(prep):
        (ssem, sqv), (csem, cvv) = prep
        P.w("tensor", ssem, sqv)
        P.w("tensor", csem, cvv)
        sf = pipe["stats_free"]
        P.w("tensor", sf[0][0], sf[0][1])
        P.w("tensor", sf[1][0], sf[1][1])
        for c in range(DC):
            P.i("tensor", lambda e, c=c:
                e.matmul(pb[6], ones_bf, xbf[:, c, :],
                         start=(c == 0), stop=(c == DC - 1)))
        pe_stats = None
        for c in range(DC):
            pe_stats = P.i("tensor", lambda e, c=c:
                           e.matmul(pb[7], ones_bf, x2b[:, c, :],
                                    start=(c == 0), stop=(c == DC - 1)),
                           inc="pe" if c == DC - 1 else None)
        pipe["xsq_pe"] = pe_stats
        return pe_stats

    def ln_math_apply(s, pe_stats, lb_gate_pe, lb_store_gq=0, xs=None, lb=None):
        m_s, m2var, sda = Sm[:, 0, :], Sm[:, 1, :], Sm[:, 2, :]
        P.w("scalar", "pe", pe_stats)
        P.w("scalar", "ve", pipe["sm_ve"])
        P.i("scalar", lambda e: e.activation(out=m_s, in_=pb[6],
                                             func=AF.Copy, scale=1.0 / D))
        scm = P.i("scalar", lambda e: e.activation(out=m2var, in_=m_s,
                                                   func=AF.Square), inc="sc")
        P.w("vector", "sc", scm)
        P.w("vector", "pe", pe_stats)
        vev = P.i("vector", lambda e:
                  e.scalar_tensor_tensor(out=m2var, in0=pb[7], scalar=1.0 / D,
                                         in1=m2var, op0=ALU.mult,
                                         op1=ALU.subtract), inc="ve")
        pipe["stats_free"] = (("sc", scm), ("ve", vev))
        P.w("scalar", "ve", vev)
        P.i("scalar", lambda e: e.activation(out=sda, in_=m2var,
                                             func=AF.Ln, bias=eps5))
        scsd = P.i("scalar", lambda e: e.activation(out=sda, in_=sda,
                                                    func=AF.Exp, scale=-0.5),
                   inc="sc")
        P.w("vector", "sc", scsd)
        P.i("vector", lambda e: e.tensor_mul(m_s, m_s, sda), inc="ve")
        xs = Xs if xs is None else xs
        P.w("vector", "pe", lb_gate_pe)
        P.w("vector", "gq", lb_store_gq)
        vap = None
        for c in range(DC):
            P.i("vector", lambda e, c=c, xs=xs:
                e.tensor_mul(m2var, xs[:, c, :], sda))
            vap = P.i("vector", lambda e, c=c, s=s, lb=(Lb if lb is None else lb):
                      e.tensor_sub(lb[:, c, s * 512:(s + 1) * 512],
                                   m2var, m_s),
                      inc="ve" if c == DC - 1 else None)
        pipe["sm_ve"] = vap
        return vap

    def emit_ln(srcap, t, x_free, lb_gate_pe, lb_store_gq=0):
        g0 = ln_load(srcap, t * TT, x_free)
        p0 = ln_prep(g0)
        st0 = ln_stats(p0)
        va0 = ln_math_apply(0, st0, lb_gate_pe, lb_store_gq)
        g1 = ln_load(srcap, t * TT + 512, ("ve", va0))
        p1 = ln_prep(g1)
        st1 = ln_stats(p1)
        return ln_math_apply(1, st1, lb_gate_pe, lb_store_gq)

    def proj_pass(wname, n_oc, n_ic, rhs_slice, evac, banks3, wready_i,
                  extra_rhs_wait=None, hook=None):
        cols = n_ic * 128
        nslot = 4 if cols <= 2048 else 2
        for oc in range(n_oc):
            wpump(8)
            if hook is not None:
                hook(oc)
            slot = oc % nslot
            b = slot % 2
            off = (slot // 2) * 2048 if nslot == 4 else 0
            qsem = "sa" if b == 0 else "sb"
            P.w("sync", "wrd", wready_i, defer=True)
            if nslot == 4:
                P.w("sync", "pe", wbuf_free2[slot])
            else:
                P.w("sync", "pe", max(wbuf_free2[b], wbuf_free2[b + 2]))
            wl = P.i("sync", lambda e, oc=oc, b=b, off=off, w=wz[wname], cols=cols:
                     e.dma_start(out=Ws[:, b, off:off + cols], in_=w[oc]),
                     inc=qsem, n=16)
            bk0, bk1 = banks3[oc % len(banks3)]
            P.w("tensor", qsem, wl)
            if extra_rhs_wait is not None:
                P.w("tensor", extra_rhs_wait[0], extra_rhs_wait[1])
            P.w("tensor", bank_evac[bk0][0], bank_evac[bk0][1])
            P.w("tensor", bank_evac[bk1][0], bank_evac[bk1][1])
            pev = None
            for i in range(n_ic):
                st, sp = (i == 0), (i == n_ic - 1)
                P.i("tensor", lambda e, i=i, b=b, off=off, st=st, sp=sp, bk=bk0:
                    e.matmul(pb[bk], Ws[:, b, off + i * 128:off + (i + 1) * 128],
                             rhs_slice(i, 0), start=st, stop=sp))
                pev = P.i("tensor", lambda e, i=i, b=b, off=off, st=st, sp=sp, bk=bk1:
                          e.matmul(pb[bk], Ws[:, b, off + i * 128:off + (i + 1) * 128],
                                   rhs_slice(i, 1), start=st, stop=sp),
                          inc="pe" if sp else None)
            bank_mm[bk0] = bank_mm[bk1] = pev
            if nslot == 4:
                wbuf_free2[slot] = pev
            else:
                wbuf_free2[b] = wbuf_free2[b + 2] = pev
            bank_evac[bk0] = bank_evac[bk1] = evac(oc, bk0 // 2)

    def ffn(srcap, dst, ve_ln0, x_free, wready):
        ve_ln = ve_ln0
        store_gq = 0
        B3P = [(0, 1), (2, 3), (4, 5)]
        for t in range(NT):
            nxt = t + 1 < NT
            stats_box = {}
            prep0 = None
            if nxt:
                g0 = ln_load(srcap, (t + 1) * TT, x_free)
                prep0 = ln_prep(g0, pool=True)

            def hook0(oc, prep=prep0, box=stats_box):
                if prep is not None and oc == 10:
                    box[0] = ln_stats(prep)

            def evac_w1(oc, g):
                P.w("scalar", "pe", bank_mm[2 * g])
                v = P.i("scalar", lambda e, oc=oc, g=g:
                        e.activation(out=A3[:, oc, :], in_=pbw[g], func=AF.Silu,
                                     scale=svb["s1"][:, oc:oc + 1]), inc="sc")
                return ("sc", v)
            proj_pass("w1", HC, DC,
                      lambda i, s: Lb[:, i, s * 512:(s + 1) * 512],
                      evac_w1, B3P, wready["w1"], ("ve", ve_ln), hook0)

            def evac_w2(oc, g):
                P.w("vector", "pe", bank_mm[2 * g])
                P.w("vector", "sc", P.cnt["sc"])
                v = P.i("vector", lambda e, oc=oc, g=g:
                        e.tensor_mul(A3[:, oc, :], pbw[g], A3[:, oc, :]),
                        inc="ve")
                return ("ve", v)
            proj_pass("w2", HC, DC,
                      lambda i, s: Lb[:, i, s * 512:(s + 1) * 512],
                      evac_w2, B3P, wready["w2"], ("ve", ve_ln))
            p_w2 = P.cnt["pe"]
            ve_h = P.cnt["ve"]
            prep1 = None
            if nxt:
                va0 = ln_math_apply(0, stats_box[0], p_w2)
                g1 = ln_load(srcap, (t + 1) * TT + 512, ("ve", va0))
                prep1 = ln_prep(g1, pool=True)

            def hook1(oc, prep=prep1, box=stats_box):
                if prep is not None and oc == 8:
                    box[1] = ln_stats(prep)

            def evac_h1(oc, g):
                P.w("scalar", "pe", bank_mm[2 * g])
                if oc >= 14:
                    P.w("scalar", "wrd", wready["wv"], defer=True)
                v = P.i("scalar", lambda e, oc=oc, g=g:
                        e.activation(out=B3[:, oc, :], in_=pbw[g], func=AF.Gelu,
                                     scale=svb["h1s"][:, oc:oc + 1]), inc="sc")
                return ("sc", v)
            proj_pass("h1", HC, HC,
                      lambda i, s: A3[:, i, s * 512:(s + 1) * 512],
                      evac_h1, [(0, 1), (2, 3), (4, 5)], wready["h1"],
                      ("ve", ve_h), hook1)
            sc_g1 = P.cnt["sc"]
            if nxt:
                ve_ln_next = ln_math_apply(1, stats_box[1], p_w2)

            def evac_h2(oc, g):
                P.w("scalar", "pe", bank_mm[2 * g])
                v = P.i("scalar", lambda e, oc=oc, g=g:
                        e.activation(out=A3[:, oc, :], in_=pbw[g], func=AF.Gelu,
                                     scale=svb["h2s"][:, oc:oc + 1]), inc="sc")
                return ("sc", v)
            proj_pass("h2", HC, HC,
                      lambda i, s: B3[:, i, s * 512:(s + 1) * 512],
                      evac_h2, [(0, 1), (2, 3), (4, 5)], wready["h2"],
                      ("sc", sc_g1))
            sc_h2 = P.cnt["sc"]

            p_h2 = P.cnt["pe"]
            for k in range(4):
                w3ring_free[k] = max(w3ring_free[k], p_h2)
            w3ring = [B3[:, 4 * k:4 * k + 4, :].rearrange("p a b -> p (a b)")
                      for k in range(4)]
            xg = ("ve", pipe["sm_ve"])
            for s in range(2):
                tc = t * TT + s * 512
                P.w("gpsimd", xg[0], xg[1])
                P.w("gpsimd", "gq", store_gq)
                ld = P.i("gpsimd", lambda e, tc=tc, sr=srcap:
                         e.dma_start(out=Xs, in_=sr[:, tc:tc + 512].rearrange(
                             "(c p) t -> p c t", p=128)), inc="gq", n=16)
                vv = None
                for oc in range(DC):
                    wpump(8)
                    slot = (s * DC + oc) % 4
                    qsem = "sa" if slot % 2 == 0 else "sb"
                    P.w("sync", "wrd", wready["w3"], defer=True)
                    P.w("sync", "pe", w3ring_free[slot])
                    wl = P.i("sync", lambda e, oc=oc, slot=slot:
                             e.dma_start(out=w3ring[slot], in_=wz["w3"][oc]),
                             inc=qsem, n=16)
                    bk = oc % 4
                    P.w("tensor", qsem, wl)
                    P.w("tensor", "sc", sc_h2)
                    P.w("tensor", bank_evac[bk][0], bank_evac[bk][1])
                    pev = None
                    for i in range(HC):
                        pev = P.i("tensor", lambda e, i=i, slot=slot, s=s, bk=bk:
                                  e.matmul(pb[bk],
                                           w3ring[slot][:, i * 128:(i + 1) * 128],
                                           A3[:, i, s * 512:(s + 1) * 512],
                                           start=(i == 0), stop=(i == HC - 1)),
                                  inc="pe" if i == HC - 1 else None)
                    bank_mm[bk] = pev
                    w3ring_free[slot] = pev
                    P.w("vector", "pe", pev)
                    P.w("vector", "gq", ld)
                    vv = P.i("vector", lambda e, oc=oc, bk=bk:
                             e.scalar_tensor_tensor(out=Xs[:, oc, :], in0=pb[bk],
                                                    scalar=halfs3[:, oc:oc + 1],
                                                    in1=Xs[:, oc, :],
                                                    op0=ALU.mult, op1=ALU.add),
                             inc="ve")
                    bank_evac[bk] = ("ve", vv)
                P.w("gpsimd", "ve", vv)
                store_gq = P.i("gpsimd", lambda e, tc=tc:
                               e.dma_start(out=dst[:, tc:tc + 512].rearrange(
                                               "(c p) t -> p c t", p=128),
                                           in_=Xs), inc="gq", n=16)
                xg = ("gq", store_gq)
            x_free = xg
            if nxt:
                ve_ln = ve_ln_next
        return {"store_gq": store_gq, "x_free": x_free}

    init_consts()
    cur_src = xT
    x_free = ("pe", 0)
    ve_ln0 = emit_ln(xT, 0, x_free, 0) if "f1" in phases else None

    if "f1" in phases:
        wpump(128)
        dst1 = x1_scr if ("b" in phases or "f2" in phases) else outT
        r = ffn(xT, dst1, ve_ln0, ("ve", ve_ln0), wready)
        x_free = r["x_free"]
        cur_src = dst1

    wpump(len(WITEMS))
    if "b" in phases:
        b_src = cur_src
        P.w("sync", "wrd", wready["wv"])
        P.w("sync", "pe", P.cnt["pe"])
        wkl = None
        for c in range(DC):
            P.i("sync", lambda e, c=c:
                e.dma_start(out=B3[:, 16 + c, :],
                            in_=wp["wk"][c * 128:(c + 1) * 128, :]), inc="sa", n=16)
            wkl = P.i("sync", lambda e, c=c:
                      e.dma_start(out=B3[:, 24 + c, :],
                                  in_=wp["wv"][c * 128:(c + 1) * 128, :]), inc="sa", n=16)
        P.w("vector", "wrd", wready["wv"])
        P.i("vector", lambda e: e.memset(vstage, 0.0), inc="ve")
        kvadd = 0
        ve_ln = emit_ln(b_src, 0, x_free, P.cnt["pe"])
        P.w("gpsimd", "ve", ve_ln)
        lnst = P.i("gpsimd", lambda e:
                   e.dma_start(out=ln_scr[:, 0:TT].rearrange(
                                   "(c p) t -> p c t", p=128),
                               in_=Lb), inc="gq", n=16)
        xs2b1 = A3.rearrange("p a b -> p (a b)")[:, 0:8192].bitcast(F32)\
            .rearrange("p (c f) -> p c f", c=DC)
        lnbufs = [Lb, A3[:, 8:16, :]]
        xs2_ve = 0
        xs2_pe0 = P.cnt["pe"]
        prev_pkv7 = xs2_pe0
        for t in range(NT):
            lnb_cur = lnbufs[t % 2]
            lnb_nxt = lnbufs[(t + 1) % 2]
            nxt = t + 1 < NT
            bbox = {}
            if nxt:
                g0 = ln_load(b_src, (t + 1) * TT, ("ve", pipe["sm_ve"]))
                if t == 0:
                    P.w("gpsimd", "pe", xs2_pe0)
                g1 = ln_load(b_src, (t + 1) * TT + 512, ("ve", xs2_ve), xs2b1)
                bbox["p0"] = ln_prep(g0, pool=True)
            k1 = v1 = None
            pkv_tl = [None] * 8
            for tl in range(8):
                if nxt and tl == 3:
                    bbox[0] = ln_stats(bbox["p0"])
                if nxt and tl == 4:
                    va0 = ln_math_apply(0, bbox[0], prev_pkv7, lnst, lb=lnb_nxt)
                    bbox["p1"] = ln_prep(g1, xs2b1, pool=True)
                if nxt and tl == 6:
                    bbox[1] = ln_stats(bbox["p1"])
                P.w("tensor", "sa", wkl)
                P.w("tensor", "ve", ve_ln)
                bko = 0
                for bk in range(bko, bko + 4):
                    P.w("tensor", bank_evac[bk][0], bank_evac[bk][1])
                pkv = None
                for i in range(DC):
                    st, sp = (i == 0), (i == DC - 1)
                    P.i("tensor", lambda e, i=i, tl=tl, st=st, sp=sp, L=lnb_cur, bk=bko:
                        e.matmul(pb[bk], L[:, i, tl * 128:(tl + 1) * 128],
                                 B3[:, 16 + i, 0:512], start=st, stop=sp))
                    P.i("tensor", lambda e, i=i, tl=tl, st=st, sp=sp, L=lnb_cur, bk=bko + 1:
                        e.matmul(pb[bk], L[:, i, tl * 128:(tl + 1) * 128],
                                 B3[:, 16 + i, 512:1024], start=st, stop=sp))
                    P.i("tensor", lambda e, i=i, tl=tl, st=st, sp=sp, L=lnb_cur, bk=bko + 2:
                        e.matmul(pb[bk], L[:, i, tl * 128:(tl + 1) * 128],
                                 B3[:, 24 + i, 0:512], start=st, stop=sp))
                    pkv = P.i("tensor", lambda e, i=i, tl=tl, st=st, sp=sp, L=lnb_cur, bk=bko + 3:
                              e.matmul(pb[bk], L[:, i, tl * 128:(tl + 1) * 128],
                                       B3[:, 24 + i, 512:1024], start=st, stop=sp),
                              inc="pe" if sp else None)
                pkv_tl[tl] = pkv
                for bk in range(bko, bko + 4):
                    bank_mm[bk] = pkv
                P.w("scalar", "pe", pkv)
                k1 = P.i("scalar", lambda e, tl=tl:
                         e.activation(out=B3[:, tl, :], in_=pbw[0],
                                      func=AF.Relu, scale=0.125), inc="sc")
                P.w("vector", "pe", pkv)
                v1 = P.i("vector", lambda e, tl=tl:
                         e.tensor_copy(B3[:, 8 + tl, :], pbw[1]), inc="ve")
                bank_evac[0] = bank_evac[1] = ("sc", k1)
                bank_evac[2] = bank_evac[3] = ("ve", v1)
            if nxt:
                ve_ln_n = ln_math_apply(1, bbox[1], prev_pkv7, lnst, xs2b1,
                                        lb=lnb_nxt)
            P.w("tensor", "sc", k1)
            P.w("tensor", "ve", v1)
            P.w("tensor", bank_evac[4][0], bank_evac[4][1])
            P.w("tensor", bank_evac[5][0], bank_evac[5][1])
            pkvm = None
            for h in range(16):
                for tl in range(8):
                    bk = 4 + h // 8
                    off = (h % 8) * 64
                    pkvm = P.i("tensor",
                               lambda e, tl=tl, h=h, bk=bk, off=off:
                               e.matmul(pb[bk][0:64, off:off + 64],
                                        B3[:, tl, h * 64:(h + 1) * 64],
                                        B3[:, 8 + tl, h * 64:(h + 1) * 64],
                                        start=(tl == 0), stop=(tl == 7),
                                        skip_group_check=True),
                               inc="pe" if (tl == 7 and h == 15) else None)
            bank_mm[4] = bank_mm[5] = pkvm
            P.w("vector", "pe", pkvm)
            P.i("vector", lambda e: e.tensor_add(vstage[0:64, 0:512],
                                                 vstage[0:64, 0:512],
                                                 pb[4][0:64, :]))
            kvadd = P.i("vector", lambda e: e.tensor_add(vstage[0:64, 512:1024],
                                                         vstage[0:64, 512:1024],
                                                         pb[5][0:64, :]), inc="ve")
            bank_evac[4] = bank_evac[5] = ("ve", kvadd)
            if nxt:
                ve_ln = ve_ln_n
                xs2_ve = ve_ln
                P.w("gpsimd", "ve", ve_ln)
                lnst = P.i("gpsimd", lambda e, t=t, L=lnb_nxt:
                           e.dma_start(
                               out=ln_scr[:, (t + 1) * TT:(t + 2) * TT].rearrange(
                                   "(c p) t -> p c t", p=128),
                               in_=L), inc="gq", n=16)
            prev_pkv7 = pkv_tl[7]
        x_free = ("ve", pipe["sm_ve"])
        wbuf_free2[0] = wbuf_free2[1] = wbuf_free2[2] = wbuf_free2[3] = pkvm
        P.w("vector", "ve", kvadd)
        P.i("vector", lambda e: e.tensor_copy(vout[0:64, :], vstage[0:64, :]))
        P.i("vector", lambda e: e.memset(kv_bd, 0.0), inc="ve")
        for j in range(8):
            P.i("vector", lambda e, j=j:
                e.tensor_copy(kv_bd[0:64, j * 128:j * 128 + 64],
                              vout[0:64, 2 * j * 64:(2 * j + 1) * 64]),
                inc="ve" if j == 7 else None)
        kfve = P.cnt["ve"]
        P.w("gpsimd", "ve", kfve)
        kfg = None
        for j in range(8):
            kfg = P.i("gpsimd", lambda e, j=j:
                      e.dma_start(out=kv_bd[64:128, j * 128 + 64:(j + 1) * 128],
                                  in_=vout[0:64, (2 * j + 1) * 64:(2 * j + 2) * 64]),
                      inc="gq", n=16)
        dst2 = x2_scr if "f2" in phases else outT
        store_gq2 = 0
        lnb2 = [Lb, B3[:, 8:16, :]]
        qu_pe_hist = [0, 0]
        for t in range(NT):
            Lc = lnb2[t % 2]
            P.w("gpsimd", "gq", kfg)
            P.w("gpsimd", "pe", qu_pe_hist[t % 2])
            lnl = []
            for h in range(4):
                lnl.append(P.i("gpsimd", lambda e, t=t, h=h, L=Lc:
                               e.dma_start(out=L[:, 2 * h:2 * h + 2, :],
                                           in_=ln_scr[h * 256:(h + 1) * 256,
                                                      t * TT:(t + 1) * TT]
                                           .rearrange("(c p) t -> p c t", p=128)),
                               inc="gq", n=16))
            quev = None
            for oc in range(DC):
                slot = oc % 4
                b = slot % 2
                off = (slot // 2) * 2048
                qsem = "sa" if b == 0 else "sb"
                P.w("sync", "wrd", wready["wu"])
                P.w("sync", "pe", wbuf_free2[slot])
                P.i("sync", lambda e, oc=oc, b=b, off=off:
                    e.dma_start(out=Ws[:, b, off:off + 1024], in_=wz["wq"][oc]),
                    inc=qsem, n=16)
                wl = P.i("sync", lambda e, oc=oc, b=b, off=off:
                         e.dma_start(out=Ws[:, b, off + 1024:off + 2048],
                                     in_=wz["wu"][oc]), inc=qsem, n=16)
                bks = [4 * (oc % 2) + k for k in range(4)]
                P.w("tensor", qsem, wl)
                P.w("tensor", "gq", lnl[-1])
                for bk in bks:
                    P.w("tensor", bank_evac[bk][0], bank_evac[bk][1])
                ppe = None
                for i in range(DC):
                    st, sp = (i == 0), (i == DC - 1)
                    P.i("tensor", lambda e, i=i, b=b, o=off, st=st, sp=sp, bk=bks[0], L=Lc:
                        e.matmul(pb[bk], Ws[:, b, o + i * 128:o + (i + 1) * 128],
                                 L[:, i, 0:512], start=st, stop=sp))
                    P.i("tensor", lambda e, i=i, b=b, o=off, st=st, sp=sp, bk=bks[1], L=Lc:
                        e.matmul(pb[bk], Ws[:, b, o + i * 128:o + (i + 1) * 128],
                                 L[:, i, 512:1024], start=st, stop=sp))
                    P.i("tensor", lambda e, i=i, b=b, o=off, st=st, sp=sp, bk=bks[2], L=Lc:
                        e.matmul(pb[bk],
                                 Ws[:, b, o + 1024 + i * 128:o + 1024 + (i + 1) * 128],
                                 L[:, i, 0:512], start=st, stop=sp))
                    ppe = P.i("tensor", lambda e, i=i, b=b, o=off, st=st, sp=sp, bk=bks[3], L=Lc:
                              e.matmul(pb[bk],
                                       Ws[:, b, o + 1024 + i * 128:o + 1024 + (i + 1) * 128],
                                       L[:, i, 512:1024], start=st, stop=sp),
                              inc="pe" if sp else None)
                for bk in bks:
                    bank_mm[bk] = ppe
                wbuf_free2[slot] = ppe
                P.w("scalar", "pe", ppe)
                P.i("scalar", lambda e, oc=oc, bk=bks[0]:
                    e.activation(out=A3[:, oc, 0:512], in_=pb[bk],
                                 func=AF.Relu, scale=0.125))
                P.i("scalar", lambda e, oc=oc, bk=bks[1]:
                    e.activation(out=A3[:, oc, 512:1024], in_=pb[bk],
                                 func=AF.Relu, scale=0.125))
                P.i("scalar", lambda e, oc=oc, bk=bks[2]:
                    e.activation(out=A3[:, 8 + oc, 0:512], in_=pb[bk],
                                 func=AF.Silu))
                quev = P.i("scalar", lambda e, oc=oc, bk=bks[3]:
                           e.activation(out=A3[:, 8 + oc, 512:1024], in_=pb[bk],
                                        func=AF.Silu), inc="sc")
                for bk in bks:
                    bank_evac[bk] = ("sc", quev)
            qu_pe_hist[t % 2] = P.cnt["pe"]
            P.w("tensor", "gq", kfg)
            x2f = x2b.rearrange("p a b -> p (a b)")
            smf = Sm.rearrange("p a b -> p (a b)")
            xsf = Xs.rearrange("p a b -> p (a b)")
            SETS = [
                {"a2": x2f[:, 0:1024], "sd": smf[:, 512:1536],
                 "t1": vstage[:, 0:1024]},
                {"a2": x2f[:, 1024:2048], "sd": xsf[:, 0:1024],
                 "t1": xsf[:, 1024:2048]},
            ]
            prev_n2 = [("pe", 0), ("pe", 0)]
            prev_anu = [0, 0]
            anu = None
            for j in range(8):
                st_ = SETS[j % 2]
                aw = pbw[j % 2]
                nw = pbw[2 + j % 2]
                P.w("tensor", "sc", quev)
                P.w("tensor", bank_evac[2 * (j % 2)][0], bank_evac[2 * (j % 2)][1])
                P.w("tensor", bank_evac[2 * (j % 2) + 1][0],
                    bank_evac[2 * (j % 2) + 1][1])
                P.i("tensor", lambda e, j=j, aw=aw:
                    e.matmul(aw[:, 0:512], kv_bd[:, j * 128:(j + 1) * 128],
                             A3[:, j, 0:512], start=True, stop=True))
                att_pe = P.i("tensor", lambda e, j=j, aw=aw:
                             e.matmul(aw[:, 512:1024],
                                      kv_bd[:, j * 128:(j + 1) * 128],
                                      A3[:, j, 512:1024],
                                      start=True, stop=True), inc="pe")
                bank_mm[2 * (j % 2)] = bank_mm[2 * (j % 2) + 1] = att_pe
                P.w("scalar", "pe", att_pe)
                P.w("scalar", prev_n2[j % 2][0], prev_n2[j % 2][1])
                sqa = P.i("scalar", lambda e, aw=aw, o=st_["a2"]:
                          e.activation(out=o, in_=aw, func=AF.Square), inc="sc")
                P.w("tensor", "sc", sqa)
                P.w("tensor", bank_evac[4 + 2 * (j % 2)][0],
                    bank_evac[4 + 2 * (j % 2)][1])
                P.w("tensor", bank_evac[5 + 2 * (j % 2) - (j % 2)][0] if False else
                    bank_evac[4 + 2 * (j % 2) + 1][0],
                    bank_evac[4 + 2 * (j % 2) + 1][1])
                P.i("tensor", lambda e, nw=nw, i_=st_["a2"]:
                    e.matmul(nw[:, 0:512], bdo, i_[:, 0:512],
                             start=True, stop=True))
                n2pe = P.i("tensor", lambda e, nw=nw, i_=st_["a2"]:
                           e.matmul(nw[:, 512:1024], bdo, i_[:, 512:1024],
                                    start=True, stop=True), inc="pe")
                bank_mm[4 + 2 * (j % 2)] = bank_mm[4 + 2 * (j % 2) + 1] = n2pe
                prev_n2[j % 2] = ("pe", n2pe)
                P.w("scalar", "pe", n2pe)
                P.w("scalar", "ve", prev_anu[j % 2])
                P.i("scalar", lambda e, nw=nw, o=st_["sd"]:
                    e.activation(out=o, in_=nw, func=AF.Ln,
                                 scale=1.0 / 64, bias=eps24))
                sda2 = P.i("scalar", lambda e, o=st_["sd"]:
                           e.activation(out=o, in_=o, func=AF.Exp, scale=-0.5),
                           inc="sc")
                bank_evac[4 + 2 * (j % 2)] = ("sc", sda2)
                bank_evac[4 + 2 * (j % 2) + 1] = ("sc", sda2)
                P.w("vector", "sc", sda2)
                P.i("vector", lambda e, aw=aw, o=st_["t1"], c=st_["sd"]:
                    e.tensor_mul(o, aw, c))
                anu = P.i("vector", lambda e, j=j, i_=st_["t1"]:
                          e.tensor_mul(A3[:, 16 + j, :], i_, A3[:, 8 + j, :]),
                          inc="ve")
                prev_anu[j % 2] = anu
                bank_evac[2 * (j % 2)] = ("ve", anu)
                bank_evac[2 * (j % 2) + 1] = ("ve", anu)
            anu_all = anu
            for s in range(2):
                tc = t * TT + s * 512
                xf = ("gq", store_gq2) if store_gq2 else x_free
                P.w("gpsimd", xf[0], xf[1])
                P.w("gpsimd", "ve", anu_all)
                lds = [P.i("gpsimd", lambda e, tc=tc:
                           e.dma_start(out=Xs,
                                       in_=b_src[:, tc:tc + 512].rearrange(
                                           "(c p) t -> p c t", p=128)),
                       inc="gq", n=16)]
                for oc in range(DC):
                    slot = (s * DC + oc) % 4
                    b = slot % 2
                    off = (slot // 2) * 2048
                    qsem = "sa" if b == 0 else "sb"
                    P.w("sync", "wrd", wready["wo"])
                    P.w("sync", "pe", wbuf_free2[slot])
                    wl = P.i("sync", lambda e, oc=oc, b=b, o=off:
                             e.dma_start(out=Ws[:, b, o:o + 1024], in_=wz["wo"][oc]),
                             inc=qsem, n=16)
                    bk = oc % 4
                    P.w("tensor", qsem, wl)
                    P.w("tensor", "ve", anu_all)
                    P.w("tensor", bank_evac[bk][0], bank_evac[bk][1])
                    pev = None
                    for i in range(DC):
                        pev = P.i("tensor", lambda e, i=i, b=b, o=off, s=s, bk=bk:
                                  e.matmul(pb[bk], Ws[:, b, o + i * 128:o + (i + 1) * 128],
                                           A3[:, 16 + i, s * 512:(s + 1) * 512],
                                           start=(i == 0), stop=(i == DC - 1)),
                                  inc="pe" if i == DC - 1 else None)
                    bank_mm[bk] = pev
                    wbuf_free2[slot] = pev
                    P.w("vector", "pe", pev)
                    P.w("vector", "gq", lds[-1])
                    vv = P.i("vector", lambda e, oc=oc, bk=bk:
                             e.tensor_add(Xs[:, oc, :], pb[bk], Xs[:, oc, :]),
                             inc="ve")
                    bank_evac[bk] = ("ve", vv)
                P.w("gpsimd", "ve", vv)
                store_gq2 = P.i("gpsimd", lambda e, tc=tc:
                                e.dma_start(out=dst2[:, tc:tc + 512].rearrange(
                                                "(c p) t -> p c t", p=128),
                                            in_=Xs), inc="gq", n=16)
        x_free = ("gq", store_gq2)
        cur_src = dst2

    if "f2" in phases:
        pipe["xsq_pe"] = P.cnt["pe"]
        pipe["stats_free"] = (("sc", P.cnt["sc"]), ("ve", P.cnt["ve"]))
        ve_ln0b = emit_ln(cur_src, 0, x_free, P.cnt["pe"])
        ffn(cur_src, outT, ve_ln0b, ("ve", ve_ln0b), wready)

    if dbg:
        BF = BF16
        dbg_ln = nc.dram_tensor("dbg_ln", [D, L], BF, kind="ExternalOutput").ap()
        dbg_x1 = nc.dram_tensor("dbg_x1", [D, L], F32, kind="ExternalOutput").ap()
        dbg_a3 = nc.dram_tensor("dbg_a3", [128, 32, 1024], BF, kind="ExternalOutput").ap()
        dbg_kv = nc.dram_tensor("dbg_kv", [128, 1024], BF, kind="ExternalOutput").ap()
        P.w("gpsimd", "gq", P.cnt["gq"])
        P.w("gpsimd", "pe", P.cnt["pe"])
        P.w("gpsimd", "ve", P.cnt["ve"])
        P.w("gpsimd", "sc", P.cnt["sc"])
        P.i("gpsimd", lambda e: e.dma_start(out=dbg_ln, in_=ln_scr), inc="gq", n=16)
        P.i("gpsimd", lambda e: e.dma_start(out=dbg_x1, in_=x1_scr), inc="gq", n=16)
        P.i("gpsimd", lambda e: e.dma_start(out=dbg_a3, in_=A3), inc="gq", n=16)
        P.i("gpsimd", lambda e: e.dma_start(out=dbg_kv, in_=kv_bd), inc="gq", n=16)
    P.w("gpsimd", "gq", P.cnt["gq"])
    P.w("gpsimd", "sa", P.cnt["sa"])
    P.w("gpsimd", "sb", P.cnt["sb"])
    for _k in range(12):
        P.w("gpsimd", f"ws{_k}", P.cnt[f"ws{_k}"])
    P.w("sync", "sa", P.cnt["sa"])
    P.w("sync", "sb", P.cnt["sb"])

    with contextlib.ExitStack() as stack:
        sh = {s: stack.enter_context(nc.semaphore(s)) for s in SEMS}
        block = stack.enter_context(nc.Block())

        def runner(name):
            def _(eng):
                for item in P.stream[name]:
                    if item[0] == "wait":
                        eng.wait_ge(sh[item[1]], item[2])
                    else:
                        _k, fn, sem, n = item
                        ins = fn(eng)
                        if sem is not None:
                            ins.then_inc(sh[sem], n)
            return _
        block.sync(runner("sync"))
        block.scalar(runner("scalar"))
        block.vector(runner("vector"))
        block.tensor(runner("tensor"))
        block.gpsimd(runner("gpsimd"))
    return nc


def _prep_inputs(inp):
    T = lambda a: np.ascontiguousarray(np.asarray(a).T)
    g = {
        "wq_t": T(inp["L0_attn_Wq"]), "wk_t": T(inp["L0_attn_Wk"]),
        "wv_t": T(inp["L0_attn_Wv"]), "wu_t": T(inp["L0_attn_Wu"]),
        "wo_t": T(inp["L0_attn_Wo"]), "w1_t": T(inp["L0_ffn_w1_w"]),
        "w2_t": T(inp["L0_ffn_w2_w"]), "h1_t": T(inp["L0_ffn_h1_w"]),
        "h2_t": T(inp["L0_ffn_h2_w"]), "w3_t": T(inp["L0_ffn_w3_w"]),
        "s1_r": T(np.asarray(inp["L0_ffn_w1_s"]).reshape(HC, 128)),
        "s2_r": T(np.asarray(inp["L0_ffn_w2_s"]).reshape(HC, 128)),
        "h1s_r": T(np.asarray(inp["L0_ffn_h1_s"]).reshape(HC, 128)),
        "h2s_r": T(np.asarray(inp["L0_ffn_h2_s"]).reshape(HC, 128)),
        "w3s_r": T(np.asarray(inp["L0_ffn_w3_s"]).reshape(DC, 128)),
    }
    return g


def kernel(**inp):
    x = np.asarray(inp["L0_attn_x"])
    assert x.shape == (NCORES, L, D)
    g = _prep_inputs(inp)
    nc = build()
    in_maps = []
    for c in range(NCORES):
        m = dict(g)
        m["xT"] = np.ascontiguousarray(x[c].T)
        in_maps.append(m)
    res = run_bass_kernel_spmd(nc, in_maps, core_ids=list(range(NCORES)))
    out = np.stack([np.asarray(res.results[c]["outT"]).T for c in range(NCORES)])
    return np.ascontiguousarray(out.astype(np.float32))
